# version 03 (final)
# baseline (speedup 1.0000x reference)
import numpy as np
import ml_dtypes
import concourse.bass as bass
import concourse.mybir as mybir
from concourse.bass_utils import run_bass_kernel_spmd
from contextlib import ExitStack

F32 = mybir.dt.float32
BF16 = mybir.dt.bfloat16
AF = mybir.ActivationFunctionType
ALU = mybir.AluOpType
AX = mybir.AxisListType

B, S, D, H, HD = 2, 8192, 1024, 4, 256
NC = 8
TOK = 2048
NT = TOK // 128
FC = D // 128
NKC = S // 128
SHIFT = -32.0
NEG = -1e30
ME, MH = 8, 4096

SLOT_KEYPOS = []
SLOT_DIAG = []
for s in range(4):
    diag = [4 * s + d for d in range(4)]
    ownpast = list(range(0, 4 * s))
    nfor = 24 if s < 2 else 48
    foreign = [16 + i for i in range(nfor)]
    SLOT_KEYPOS.append(diag + ownpast + foreign)
    SLOT_DIAG.append(4)


class Sem:
    def __init__(self, handle):
        self.h = handle
        self.n = 0

    def take(self, k=1):
        self.n += k
        return self.n


class Prog:
    ENGS = ("sync", "scalar", "vector", "tensor", "gpsimd")

    def __init__(self, nc, es):
        self.nc = nc
        self.es = es
        self.q = {e: [] for e in self.ENGS}
        self.scopes = []

    def open_scope(self):
        s = ExitStack()
        self.scopes.append(s)
        return s

    def close_scope(self, s):
        assert self.scopes and self.scopes[-1] is s, "scope close out of order"
        self.scopes.pop()
        s.close()

    def close_all_scopes(self):
        while self.scopes:
            self.scopes.pop().close()

    def sem(self, name):
        return Sem(self.es.enter_context(self.nc.semaphore(name)))

    def sb(self, name, shape, dt, scope=None, **kw):
        scope = scope if scope is not None else self.es
        return scope.enter_context(self.nc.sbuf_tensor(name, shape, dt, **kw))

    def ps(self, name, shape, dt):
        return self.es.enter_context(self.nc.psum_tensor(name, shape, dt))

    def emit(self, eng, fn):
        self.q[eng].append(fn)

    def barrier(self, sB, engs=None):
        engs = engs or list(self.ENGS)
        targets = {e: sB[e].take(1) for e in engs}
        for e in engs:
            def fn(eng, e=e, targets=dict(targets)):
                eng.sem_inc(sB[e].h, 1)
                for o, t in targets.items():
                    if o != e:
                        eng.wait_ge(sB[o].h, t)
            self.emit(e, fn)

    def finalize(self, block):
        for ename in self.ENGS:
            fns = self.q[ename]
            getattr(block, ename)(lambda eng, fns=fns: [f(eng) for f in fns])


def mm_chain(e, out_ap, pairs, sem=None, inc=1):
    n = len(pairs)
    last = None
    for i, (lhsT, rhs) in enumerate(pairs):
        last = e.matmul(out=out_ap, lhsT=lhsT, rhs=rhs,
                        start=(i == 0), stop=(i == n - 1))
    if sem is not None:
        last.then_inc(sem.h, inc)
    return last


def build_launch_A(dbg=(), stop_after="all"):
    nc = bass.Bass("TRN2", target_bir_lowering=False, debug=False)

    x0b = nc.dram_tensor("x0b", [S, D], F32, kind="ExternalInput").ap()
    wqT = nc.dram_tensor("wqT", [D, D], F32, kind="ExternalInput").ap()
    wkT = nc.dram_tensor("wkT", [D, D], F32, kind="ExternalInput").ap()
    wvT = nc.dram_tensor("wvT", [D, D], F32, kind="ExternalInput").ap()
    woT = nc.dram_tensor("woT", [D, D], F32, kind="ExternalInput").ap()
    bo = nc.dram_tensor("bo", [1, D], F32, kind="ExternalInput").ap()
    betas = nc.dram_tensor("betas", [1, H], F32, kind="ExternalInput").ap()
    memz = nc.dram_tensor("memz", [H, HD, HD + 1], F32, kind="ExternalInput").ap()
    wg = nc.dram_tensor("wg", [D, 8], F32, kind="ExternalInput").ap()
    bgv = nc.dram_tensor("bgv", [1, 8], F32, kind="ExternalInput").ap()
    w1 = nc.dram_tensor("w1", [ME, D, MH], F32, kind="ExternalInput").ap()
    b1 = nc.dram_tensor("b1", [ME, MH], F32, kind="ExternalInput").ap()
    w2 = nc.dram_tensor("w2", [ME, MH, D], F32, kind="ExternalInput").ap()
    b2 = nc.dram_tensor("b2", [ME, D], F32, kind="ExternalInput").ap()
    wf_bd = nc.dram_tensor("wf_bd", [FC, 128, 128], F32, kind="ExternalInput").ap()
    wi_bd = nc.dram_tensor("wi_bd", [FC, 128, 128], F32, kind="ExternalInput").ap()
    wh_bd = nc.dram_tensor("wh_bd", [FC, 128, 128], F32, kind="ExternalInput").ap()
    l1wT = nc.dram_tensor("l1wT", [D, D], F32, kind="ExternalInput").ap()
    mask_in = nc.dram_tensor("mask_in", [128, 896], F32, kind="ExternalInput").ap()
    bias128 = nc.dram_tensor("bias128", [128, 256], F32, kind="ExternalInput").ap()
    bg128 = nc.dram_tensor("bg128", [128, 8], F32, kind="ExternalInput").ap()
    bo128 = nc.dram_tensor("bo128", [128, D], F32, kind="ExternalInput").ap()
    b2bc_in = nc.dram_tensor("b2bc_in", [128, ME * D], F32, kind="ExternalInput").ap()
    eye128 = nc.dram_tensor("eye128", [128, 128], F32, kind="ExternalInput").ap()

    y_out = nc.dram_tensor("y_out", [TOK, D], F32, kind="ExternalOutput").ap()
    f_out = nc.dram_tensor("f_out", [D, TOK], F32, kind="ExternalOutput").ap()
    v_out = nc.dram_tensor("v_out", [D, TOK], F32, kind="ExternalOutput").ap()

    xn_tm = nc.dram_tensor("xn_tm", [S, D], BF16)
    xn_fm = nc.dram_tensor("xn_fm", [D, S], BF16)
    kfm_d = nc.dram_tensor("kfm_d", [H, HD, S], BF16)
    vtm_d = nc.dram_tensor("vtm_d", [H, S, HD], BF16)
    xn1_tm = nc.dram_tensor("xn1_tm", [TOK, D], BF16)
    xn1_fm = nc.dram_tensor("xn1_fm", [D, TOK], BF16)
    attcat_d = nc.dram_tensor("attcat_d", [D, TOK], BF16)

    dbg_out = {}
    for name, shape, dt in dbg:
        dbg_out[name] = nc.dram_tensor("dbg_" + name, list(shape), dt,
                                       kind="ExternalOutput").ap()

    es = ExitStack()
    with es:
        p = Prog(nc, es)
        sLd = p.sem("sLd")
        sWt = p.sem("sWt")
        sSt = p.sem("sSt")
        sMM = p.sem("sMM")
        sAc = p.sem("sAc")
        sVe = p.sem("sVe")
        sB = {e: p.sem("sB_" + e) for e in Prog.ENGS}
        sW0 = p.sem("sW0")
        sWkv = p.sem("sWkv")
        sMz = p.sem("sMz")
        sW4 = p.sem("sW4")
        sWmoe = p.sem("sWmoe")
        sB1 = p.sem("sB1")
        sBd = p.sem("sBd")
        sLdA = p.sem("sLdA")
        sLdB = p.sem("sLdB")

        ones_col = p.sb("ones_col", [128, 1], BF16)
        ones_row = p.sb("ones_row", [1, 128], BF16)
        ones_row32 = p.sb("ones_row32", [1, 128], F32)
        shift_col = p.sb("shift_col", [128, 1], F32)
        eps_col = p.sb("eps_col", [128, 1], F32)
        gcols = p.sb("gcols", [128, 2 * H], F32)
        bias_bc = p.sb("bias_bc", [128, 256], F32)
        bg_bc = p.sb("bg_bc", [128, 8], F32)
        sel_all = p.sb("sel_all", [128, NT * 8], F32)
        b2_sb = p.sb("b2_sb", [8, D], BF16)
        wg_sb = p.sb("wg_sb", [128, FC * 8], BF16)
        ident128 = p.sb("ident128", [128, 128], BF16)
        tinys = p.sb("tinys", [128, 64], F32)
        y_acc = p.sb("y_acc", [128, NT * D], F32)
        IOTA, BIASR, GPACK, BGR = 0, 1024, 1536, 1600

        pb = [p.ps(f"pb{i}", [128, 512], F32) for i in range(7)]
        pbt = p.ps("pbt", [128, 128], BF16)

        att_es = p.open_scope()
        mask_big = p.sb("mask_big", [128, 896], F32, scope=att_es)
        q_own = [[p.sb(f"qo{h}_{d}", [128, TOK], BF16, scope=att_es)
                  for d in range(2)] for h in range(H)]

        t_ld0 = sLd.take(16 * 3)

        def p0_sync(e):
            e.dma_start(out=mask_big[:], in_=mask_in).then_inc(sLd.h, 16)
            e.dma_start(out=bias_bc[:], in_=bias128).then_inc(sLd.h, 16)
            e.dma_start(out=bg_bc[:], in_=bg128).then_inc(sLd.h, 16)
        p.emit("sync", p0_sync)

        t_w0 = sW0.take(16 * (3 + FC))

        def p0_g(e):
            e.dma_start(out=ident128[:], in_=eye128).then_inc(sW0.h, 16)
            e.dma_start(out=tinys[0:1, 8:12], in_=betas).then_inc(sW0.h, 16)
            e.dma_start(out=b2_sb[:], in_=b2).then_inc(sW0.h, 16)
            for i in range(FC):
                e.dma_start(out=wg_sb[:, i * 8:(i + 1) * 8],
                            in_=wg[i * 128:(i + 1) * 128, :]).then_inc(sW0.h, 16)
        p.emit("gpsimd", p0_g)

        t_ve0 = sVe.take(1)

        def p0_v0(e):
            e.memset(ones_col[:], 1.0)
            e.memset(ones_row[:], 1.0)
            e.memset(ones_row32[:], 1.0)
            e.memset(shift_col[:], SHIFT)
            e.memset(eps_col[:], 1e-5).then_inc(sVe.h, 1)
        p.emit("vector", p0_v0)

        t_mm0 = sMM.take(1)

        def p0_t0(e):
            e.wait_ge(sW0.h, t_w0)
            e.wait_ge(sVe.h, t_ve0)
            e.matmul(out=pb[0][:, 0:H], lhsT=ones_row32[:],
                     rhs=tinys[0:1, 8:12], start=True,
                     stop=True).then_inc(sMM.h, 1)
        p.emit("tensor", p0_t0)

        t_ac0 = sAc.take(1)

        def p0_s0(e):
            e.wait_ge(sMM.h, t_mm0)
            e.activation(out=gcols[:, 0:H], in_=pb[0][:, 0:H],
                         func=AF.Sigmoid).then_inc(sAc.h, 1)
        p.emit("scalar", p0_s0)

        t_ve1 = sVe.take(1)

        def p0_v1(e):
            e.wait_ge(sAc.h, t_ac0)
            e.tensor_scalar(out=gcols[:, H:2 * H], in0=gcols[:, 0:H],
                            scalar1=-1.0, scalar2=1.0, op0=ALU.mult,
                            op1=ALU.add).then_inc(sVe.h, 1)
        p.emit("vector", p0_v1)

        if "maskbig" in dbg_out:
            t = sSt.take(16 * 3)

            def fn_dbgm(e, tv=sVe.n, ta=sAc.n, tm=sMM.n):
                e.wait_ge(sVe.h, tv)
                e.wait_ge(sAc.h, ta)
                e.wait_ge(sMM.h, tm)
                e.dma_start(out=dbg_out["maskbig"], in_=mask_big[:]).then_inc(sSt.h, 16)
                e.dma_start(out=dbg_out["biasbc"], in_=bias_bc[:]).then_inc(sSt.h, 16)
                e.dma_start(out=dbg_out["gcolsd"], in_=gcols[:]).then_inc(sSt.h, 16)
            p.emit("sync", fn_dbgm)

        p.barrier(sB)
        if stop_after == "p0":
            return _finish(nc, p, es, sSt, att_es, sorted(dbg_out.keys()))

        p2_es = p.open_scope()
        wkv = {nm: [[p.sb(f"{nm}h{h}_{i}", [128, HD], BF16, scope=p2_es)
                     for i in range(FC)] for h in range(H)]
               for nm in ("wq", "wk", "wv")}
        wsrc = {"wq": wqT, "wk": wkT, "wv": wvT}
        t_wkv_all = sWkv.take(16 * FC * 3 * H)

        def p2_w(e):
            for h in range(H):
                for nm in ("wq", "wk", "wv"):
                    for i in range(FC):
                        e.dma_start(out=wkv[nm][h][i][:],
                                    in_=wsrc[nm][i * 128:(i + 1) * 128,
                                                 h * HD:(h + 1) * HD]).then_inc(sWkv.h, 16)
        p.emit("gpsimd", p2_w)

        p1_es = p.open_scope()
        ln_x = p.sb("ln_x", [128, 4 * D], F32, scope=p1_es)
        ln_o = p.sb("ln_o", [128, 4 * D], BF16, scope=p1_es)
        ln_cols = p.sb("ln_cols", [128, 32], F32, scope=p1_es)

        def emit_ln(src_dram, dst_dram, row_list):
            t_st_prev = sSt.n
            for g0 in range(0, len(row_list), 4):
                grp = row_list[g0:g0 + 4]
                n = len(grp)
                t_ld = sLd.take(16 * n)

                def fn_ld(e, grp=list(grp), t_guard=t_st_prev):
                    e.wait_ge(sSt.h, t_guard)
                    for ti, r in enumerate(grp):
                        e.dma_start(out=ln_x[:, ti * D:(ti + 1) * D],
                                    in_=src_dram[r:r + 128, :]).then_inc(sLd.h, 16)
                p.emit("sync", fn_ld)

                t_ac = sAc.take(1)

                def fn_s(e, n=n, t_ld=t_ld):
                    e.wait_ge(sLd.h, t_ld)
                    SUM, SSQ, M, M2, V, RS, BC, SC = range(8)

                    def cc(ti, kk):
                        return ln_cols[:, ti * 8 + kk:ti * 8 + kk + 1]
                    for ti in range(n):
                        xt = ln_x[:, ti * D:(ti + 1) * D]
                        e.activation(out=ln_o[:, ti * D:(ti + 1) * D], in_=xt,
                                     func=AF.Square, accum_out=cc(ti, SSQ))
                        e.activation(out=ln_o[:, ti * D:(ti + 1) * D], in_=xt,
                                     func=AF.Identity, accum_out=cc(ti, SUM))
                    for ti in range(n):
                        e.mul(out=cc(ti, M), in_=cc(ti, SUM), mul=1.0 / D)
                    for ti in range(n):
                        e.activation(out=cc(ti, M2), in_=cc(ti, M), func=AF.Square)
                    for ti in range(n):
                        e.activation(out=cc(ti, SC), in_=cc(ti, M2),
                                     func=AF.Identity, scale=-1.0, bias=eps_col[:])
                    for ti in range(n):
                        e.activation(out=cc(ti, V), in_=cc(ti, SSQ),
                                     func=AF.Identity, scale=1.0 / D, bias=cc(ti, SC))
                    for ti in range(n):
                        e.activation(out=cc(ti, V), in_=cc(ti, V), func=AF.Ln)
                    for ti in range(n):
                        e.activation(out=cc(ti, RS), in_=cc(ti, V), func=AF.Exp,
                                     scale=-0.5)
                    for ti in range(n):
                        e.activation(out=cc(ti, SC), in_=cc(ti, M),
                                     func=AF.Identity, scale=cc(ti, RS))
                    for ti in range(n):
                        e.mul(out=cc(ti, BC), in_=cc(ti, SC), mul=-1.0)
                    last = None
                    for ti in range(n):
                        last = e.activation(out=ln_o[:, ti * D:(ti + 1) * D],
                                            in_=ln_x[:, ti * D:(ti + 1) * D],
                                            func=AF.Identity, scale=cc(ti, RS),
                                            bias=cc(ti, BC))
                    last.then_inc(sAc.h, 1)
                p.emit("scalar", fn_s)

                t_st = sSt.take(16 * n)

                def fn_st(e, grp=list(grp), t_ac=t_ac):
                    e.wait_ge(sAc.h, t_ac)
                    for ti, r in enumerate(grp):
                        e.dma_start(out=dst_dram[r:r + 128, :],
                                    in_=ln_o[:, ti * D:(ti + 1) * D]).then_inc(sSt.h, 16)
                p.emit("sync", fn_st)
                t_st_prev = t_st
            return t_st_prev

        t_ln0 = emit_ln(x0b, xn_tm.ap(), [r * 128 for r in range(NKC)])

        if "xn_tm" in dbg_out:
            t = sSt.take(16)

            def fn_dbg(e, tw=t_ln0):
                e.wait_ge(sSt.h, tw)
                e.dma_start(out=dbg_out["xn_tm"], in_=xn_tm.ap()).then_inc(sSt.h, 16)
            p.emit("sync", fn_dbg)

        p.close_scope(p1_es)
        p.barrier(sB)
        if stop_after == "p1":
            return _finish(nc, p, es, sSt, att_es, sorted(dbg_out.keys()))

        p1b_es = p.open_scope()
        tbuf = p.sb("tbuf", [128, S], BF16, scope=p1b_es, is_transpose_dest=True)
        for i in range(FC):
            t_tr = sLd.take(16)

            def fn_tr(e, i=i, t_tr=t_tr):
                e.dma_start(out=tbuf[:], in_=xn_tm.ap()[:, i * 128:(i + 1) * 128],
                            transpose=True).then_inc(sLd.h, 16)
            p.emit("sync", fn_tr)
            t_st = sSt.take(16)

            def fn_wb(e, i=i, t_tr=t_tr):
                e.wait_ge(sLd.h, t_tr)
                e.dma_start(out=xn_fm.ap()[i * 128:(i + 1) * 128, :],
                            in_=tbuf[:]).then_inc(sSt.h, 16)
            p.emit("sync", fn_wb)

            def fn_guard(e, t_st=t_st):
                e.wait_ge(sSt.h, t_st)
            p.emit("sync", fn_guard)

        p.close_scope(p1b_es)
        p.barrier(sB)
        if stop_after == "p1b":
            return _finish(nc, p, es, sSt, att_es, sorted(dbg_out.keys()))

        xbufs = [[p.sb(f"xsA{i}", [128, 512], BF16, scope=p2_es) for i in range(FC)],
                 [p.sb(f"xsB{i}", [128, 512], BF16, scope=p2_es) for i in range(FC)]]
        ev2 = p.sb("ev2", [128, 4 * 512], BF16, scope=p2_es)

        gseq_all = []
        mm_after_tc = {}
        for tc in range(16):
            for h in range(H):
                for oc in range(2):
                    gseq_all.append((tc, "k", h, oc))
                for tt in range(4):
                    gseq_all.append((tc, "v", h, tt))
                if tc < 4:
                    for oc in range(2):
                        gseq_all.append((tc, "q", h, oc))
            mm_after_tc[tc] = len(gseq_all)

        t_mm_base = sMM.n
        t_ac_base = sAc.n
        ld_target = {}
        ld_sems = {}
        for tc in range(16):
            t_prev_cons = (t_mm_base + mm_after_tc[tc - 2]) if tc >= 2 else 0
            psem = sLdA if tc % 2 == 0 else sLdB
            t_ld = psem.take(16 * FC)
            ld_target[tc] = t_ld
            ld_sems[tc] = psem

            def fn_ld(e, tc=tc, t_prev=t_prev_cons, psem=psem):
                if t_prev:
                    e.wait_ge(sMM.h, t_prev)
                buf = xbufs[tc % 2]
                for i in range(FC):
                    e.dma_start(out=buf[i][:],
                                in_=xn_fm.ap()[i * 128:(i + 1) * 128,
                                               tc * 512:(tc + 1) * 512]).then_inc(psem.h, 16)
            p.emit("sync", fn_ld)

        st_after = {}
        for gi, (tc, kind, h, idx) in enumerate(gseq_all):
            bank = pb[gi % 2]
            t_mm = sMM.take(1)

            def fn_t(e, tc=tc, kind=kind, h=h, idx=idx, bank=bank, gi=gi,
                     t_ld=ld_target[tc], psem=ld_sems[tc], first=(gi == 0)):
                if first:
                    e.wait_ge(sWkv.h, t_wkv_all)
                e.wait_ge(psem.h, t_ld)
                if gi >= 2:
                    e.wait_ge(sAc.h, t_ac_base + gi - 1)
                buf = xbufs[tc % 2]
                if kind == "k":
                    pairs = [(wkv["wk"][h][i][:, idx * 128:(idx + 1) * 128],
                              buf[i][:]) for i in range(FC)]
                    mm_chain(e, bank[:], pairs, sMM, 1)
                elif kind == "q":
                    pairs = [(wkv["wq"][h][i][:, idx * 128:(idx + 1) * 128],
                              buf[i][:]) for i in range(FC)]
                    mm_chain(e, bank[:], pairs, sMM, 1)
                else:
                    pairs = [(buf[i][:, idx * 128:(idx + 1) * 128],
                              wkv["wv"][h][i][:]) for i in range(FC)]
                    mm_chain(e, bank[:, 0:HD], pairs, sMM, 1)
            p.emit("tensor", fn_t)

            t_slot_guard = st_after.get(gi - 1, 0)
            t_ac = sAc.take(1)
            if kind != "q":
                t_st = sSt.take(16)
                st_after[gi] = t_st
            else:
                st_after[gi] = st_after.get(gi - 1, 0)

            def fn_ev_st(e, kind=kind, h=h, idx=idx, tc=tc, bank=bank, gi=gi,
                         t_mm=t_mm, t_guard=t_slot_guard):
                e.wait_ge(sMM.h, t_mm)
                slot = gi % 4
                if kind == "q":
                    e.copy(out=q_own[h][idx][:, tc * 512:(tc + 1) * 512],
                           in_=bank[:]).then_inc(sAc.h, 1)
                    return
                if t_guard:
                    e.wait_ge(sSt.h, t_guard)
                if kind == "k":
                    e.copy(out=ev2[:, slot * 512:slot * 512 + 512],
                           in_=bank[:]).then_inc(sAc.h, 1)
                    e.dma_start(out=kfm_d.ap()[h, idx * 128:(idx + 1) * 128,
                                               tc * 512:(tc + 1) * 512],
                                in_=ev2[:, slot * 512:slot * 512 + 512]).then_inc(sSt.h, 16)
                else:
                    e.copy(out=ev2[:, slot * 512:slot * 512 + HD],
                           in_=bank[:, 0:HD]).then_inc(sAc.h, 1)
                    e.dma_start(out=vtm_d.ap()[h, tc * 512 + idx * 128:
                                               tc * 512 + (idx + 1) * 128, :],
                                in_=ev2[:, slot * 512:slot * 512 + HD]).then_inc(sSt.h, 16)
            p.emit("scalar", fn_ev_st)

        p.close_scope(p2_es)
        p.barrier(sB)

        if "kfm" in dbg_out:
            t = sSt.take(16)

            def fn_dbgk(e, tw=sSt.n - 16):
                e.wait_ge(sSt.h, tw)
                e.dma_start(out=dbg_out["kfm"], in_=kfm_d.ap()).then_inc(sSt.h, 16)
            p.emit("sync", fn_dbgk)
        if "vtm" in dbg_out:
            t = sSt.take(16)

            def fn_dbgv(e, tw=sSt.n - 16):
                e.wait_ge(sSt.h, tw)
                e.dma_start(out=dbg_out["vtm"], in_=vtm_d.ap()).then_inc(sSt.h, 16)
            p.emit("sync", fn_dbgv)
        if "qown" in dbg_out:
            for h in range(H):
                for dch in range(2):
                    t = sSt.take(16)

                    def fn_dbgq(e, h=h, dch=dch):
                        e.dma_start(
                            out=dbg_out["qown"][(h * 2 + dch) * 128:
                                                (h * 2 + dch + 1) * 128, :],
                            in_=q_own[h][dch][:]).then_inc(sSt.h, 16)
                    p.emit("sync", fn_dbgq)

        p3_es = p.open_scope()
        kres = [p.sb(f"kres{d}", [128, S], BF16, scope=p3_es) for d in range(2)]
        vres = p.sb("vres", [128, NKC * HD], BF16, scope=p3_es)
        memz_sb = [p.sb(f"memz{d}", [128, HD + 1], BF16, scope=p3_es)
                   for d in range(2)]
        pt_sb = [p.sb(f"pt{i}", [128, 512], BF16, scope=p3_es) for i in range(2)]
        smask = p.sb("smask", [128, 512], F32, scope=p3_es)
        sq_sb = [p.sb(f"sq{d}", [128, 512], BF16, scope=p3_es) for d in range(2)]
        sq32 = p.sb("sq32", [128, 512], F32, scope=p3_es)
        sq32b = p.sb("sq32b", [128, 512], F32, scope=p3_es)
        rsum_bc = p.sb("rsum_bc", [128, 512], F32, scope=p3_es)
        rden_bc = p.sb("rden_bc", [128, 512], F32, scope=p3_es)
        am_sb = p.sb("am_sb", [128, 512], F32, scope=p3_es)
        ac_st = [p.sb(f"ac_st{i}", [128, 512], BF16, scope=p3_es)
                 for i in range(2)]
        recs = p.sb("recs", [1, 2048], F32, scope=p3_es)
        RSUM, RRS, RDEN, RRD = 0, 512, 1024, 1536

        t_head_end = sMM.n
        acst_store = {}
        slot_cmb_guard = [0]

        for h in range(H):
            t_kv = sLd.take(16 * 3)
            t_mz = sMz.take(16 * 2)

            def fn_kv(e, h=h, t_guard=t_head_end):
                e.wait_ge(sMM.h, t_guard)
                e.dma_start(out=kres[0][:], in_=kfm_d.ap()[h, 0:128, :]).then_inc(sLd.h, 16)
                e.dma_start(out=kres[1][:], in_=kfm_d.ap()[h, 128:256, :]).then_inc(sLd.h, 16)
                e.dma_start(
                    out=vres[:].rearrange("p (kc d) -> p kc d", kc=NKC),
                    in_=vtm_d.ap()[h].rearrange("(kc p) d -> p kc d", p=128),
                ).then_inc(sLd.h, 16)
            p.emit("sync", fn_kv)

            def fn_mz(e, h=h, t_guard=t_head_end):
                e.wait_ge(sMM.h, t_guard)
                e.dma_start(out=memz_sb[0][:], in_=memz[h, 0:128, :]).then_inc(sMz.h, 16)
                e.dma_start(out=memz_sb[1][:], in_=memz[h, 128:256, :]).then_inc(sMz.h, 16)
            p.emit("gpsimd", fn_mz)

            for s in range(4):
                keypos = SLOT_KEYPOS[s]
                npos = len(keypos)
                qk_t = {}
                exp_t = {}
                pv_t = {}
                sm_t = {}

                def emit_qk(pi):
                    kv = keypos[pi]
                    qk_t[pi] = sMM.take(1)

                    def fn(e, pi=pi, kv=kv, h=h, s=s,
                           t_ld=t_kv, t_bfree=exp_t.get(pi - 2, 0)):
                        e.wait_ge(sLd.h, t_ld)
                        if t_bfree:
                            e.wait_ge(sAc.h, t_bfree)
                        pairs = [(kres[d][:, kv * 128:(kv + 1) * 128],
                                  q_own[h][d][:, s * 512:(s + 1) * 512])
                                 for d in range(2)]
                        mm_chain(e, pb[pi % 2][:], pairs, sMM, 1)
                    p.emit("tensor", fn)

                def emit_exp(pi):
                    if pi < 4:
                        sm_t[pi] = sVe.take(1)

                        def fnv(e, pi=pi, t_qk=qk_t[pi],
                                t_prev_exp=exp_t.get(pi - 1, 0)):
                            e.wait_ge(sMM.h, t_qk)
                            if t_prev_exp:
                                e.wait_ge(sAc.h, t_prev_exp)
                            off = 384 - 128 * pi
                            e.tensor_tensor(out=smask[:], in0=pb[pi % 2][:],
                                            in1=mask_big[:, off:off + 512],
                                            op=ALU.add).then_inc(sVe.h, 1)
                        p.emit("vector", fnv)
                    exp_t[pi] = sAc.take(1)

                    def fns(e, pi=pi, s=s, t_qk=qk_t[pi],
                            t_sm=sm_t.get(pi, 0),
                            t_ptfree=pv_t.get(pi - 2, 0)):
                        if t_ptfree:
                            e.wait_ge(sMM.h, t_ptfree)
                        if pi < 4:
                            e.wait_ge(sVe.h, t_sm)
                            src = smask[:]
                        else:
                            e.wait_ge(sMM.h, t_qk)
                            src = pb[pi % 2][:]
                        e.activation(out=pt_sb[pi % 2][:], in_=src, func=AF.Exp,
                                     bias=bias_bc[:, s * 64 + pi:s * 64 + pi + 1]
                                     ).then_inc(sAc.h, 1)
                    p.emit("scalar", fns)

                def emit_pv(pi):
                    kv = keypos[pi]
                    pv_t[pi] = sMM.take(1)

                    def fn(e, pi=pi, kv=kv, npos=npos, t_exp=exp_t[pi],
                           t_slot=slot_cmb_guard[0]):
                        e.wait_ge(sAc.h, t_exp)
                        if pi == 0 and t_slot:
                            e.wait_ge(sVe.h, t_slot)
                        st0 = (pi == 0)
                        sp = (pi == npos - 1)
                        for dc in range(2):
                            e.matmul(out=pb[2 + dc][:],
                                     lhsT=vres[:, kv * HD + dc * 128:
                                               kv * HD + (dc + 1) * 128],
                                     rhs=pt_sb[pi % 2][:], start=st0, stop=sp)
                        e.matmul(out=pb[4][0:1, :], lhsT=ones_col[:],
                                 rhs=pt_sb[pi % 2][:], start=st0,
                                 stop=sp).then_inc(sMM.h, 1)
                    p.emit("tensor", fn)

                emit_qk(0)
                emit_exp(0)
                if npos > 1:
                    emit_qk(1)
                    emit_exp(1)
                for pi in range(npos):
                    emit_pv(pi)
                    if pi + 2 < npos:
                        emit_qk(pi + 2)
                        emit_exp(pi + 2)

                t_pv_last = pv_t[npos - 1]
                t_ve_sum = sVe.take(1)

                def fn_sum_ev(e, t=t_pv_last):
                    e.wait_ge(sMM.h, t)
                    e.tensor_copy(out=recs[:, RSUM:RSUM + 512],
                                  in_=pb[4][0:1, :]).then_inc(sVe.h, 1)
                p.emit("vector", fn_sum_ev)

                t_ac_rs = sAc.take(1)

                def fn_rs(e, t=t_ve_sum):
                    e.wait_ge(sVe.h, t)
                    e.activation(out=recs[:, RRS:RRS + 512],
                                 in_=recs[:, RSUM:RSUM + 512], func=AF.Ln)
                    e.activation(out=recs[:, RRS:RRS + 512],
                                 in_=recs[:, RRS:RRS + 512], func=AF.Exp,
                                 scale=-1.0).then_inc(sAc.h, 1)
                p.emit("scalar", fn_rs)

                t_ve_sq0 = sVe.take(2)

                def fn_sq_v(e, h=h, s=s, t_guard=t_pv_last):
                    e.wait_ge(sMM.h, t_guard)
                    e.tensor_scalar(out=sq32[:], in0=q_own[h][0][:, s * 512:(s + 1) * 512],
                                    scalar1=0.0, scalar2=None, op0=ALU.min)
                    e.tensor_scalar(out=sq32b[:], in0=q_own[h][1][:, s * 512:(s + 1) * 512],
                                    scalar1=0.0, scalar2=None,
                                    op0=ALU.min).then_inc(sVe.h, 2)
                p.emit("vector", fn_sq_v)

                t_ac_sq = sAc.take(2)

                def fn_sq_s(e, t=t_ve_sq0):
                    e.wait_ge(sVe.h, t)
                    e.activation(out=sq32[:], in_=sq32[:], func=AF.Exp)
                    e.activation(out=sq32b[:], in_=sq32b[:],
                                 func=AF.Exp).then_inc(sAc.h, 2)
                p.emit("scalar", fn_sq_s)

                t_ve_sq1 = sVe.take(2)

                def fn_sq_v2(e, h=h, s=s, t=t_ac_sq):
                    e.wait_ge(sAc.h, t)
                    e.scalar_tensor_tensor(out=sq_sb[0][:],
                                           in0=q_own[h][0][:, s * 512:(s + 1) * 512],
                                           scalar=1.0, in1=sq32[:],
                                           op0=ALU.add, op1=ALU.max)
                    e.scalar_tensor_tensor(out=sq_sb[1][:],
                                           in0=q_own[h][1][:, s * 512:(s + 1) * 512],
                                           scalar=1.0, in1=sq32b[:],
                                           op0=ALU.add, op1=ALU.max).then_inc(sVe.h, 2)
                p.emit("vector", fn_sq_v2)

                t_mm_bs = sMM.take(1)

                def fn_bs(e, t=t_ac_rs, t_sq=t_ve_sq1):
                    e.wait_ge(sAc.h, t)
                    e.wait_ge(sVe.h, t_sq)
                    e.matmul(out=pb[4][:], lhsT=ones_row32[:],
                             rhs=recs[:, RRS:RRS + 512], start=True,
                             stop=True).then_inc(sMM.h, 1)
                p.emit("tensor", fn_bs)

                t_ve_bs = sVe.take(1)

                def fn_bs_ev(e, t=t_mm_bs):
                    e.wait_ge(sMM.h, t)
                    e.tensor_copy(out=rsum_bc[:], in_=pb[4][:]).then_inc(sVe.h, 1)
                p.emit("vector", fn_bs_ev)

                t_mm_den = sMM.take(1)

                def fn_den(e, t_sq=t_ve_sq1, t_mz=t_mz):
                    e.wait_ge(sVe.h, t_sq)
                    e.wait_ge(sMz.h, t_mz)
                    pairs = [(memz_sb[d][:, HD:HD + 1], sq_sb[d][:])
                             for d in range(2)]
                    mm_chain(e, pb[6][0:1, :], pairs, sMM, 1)
                p.emit("tensor", fn_den)

                t_ve_den = sVe.take(1)

                def fn_den_ev(e, t=t_mm_den):
                    e.wait_ge(sMM.h, t)
                    e.tensor_scalar(out=recs[:, RDEN:RDEN + 512],
                                    in0=pb[6][0:1, :], scalar1=1e-6,
                                    scalar2=None, op0=ALU.add).then_inc(sVe.h, 1)
                p.emit("vector", fn_den_ev)

                t_ac_rd = sAc.take(1)

                def fn_rd(e, t=t_ve_den):
                    e.wait_ge(sVe.h, t)
                    e.activation(out=recs[:, RRD:RRD + 512],
                                 in_=recs[:, RDEN:RDEN + 512], func=AF.Abs)
                    e.activation(out=recs[:, RRD:RRD + 512],
                                 in_=recs[:, RRD:RRD + 512], func=AF.Ln)
                    e.activation(out=recs[:, RRD:RRD + 512],
                                 in_=recs[:, RRD:RRD + 512], func=AF.Exp,
                                 scale=-1.0)
                    e.activation(out=recs[:, RDEN:RDEN + 512],
                                 in_=recs[:, RDEN:RDEN + 512],
                                 func=AF.Sign).then_inc(sAc.h, 1)
                p.emit("scalar", fn_rd)

                t_ve_rd = sVe.take(1)

                def fn_rd2(e, t=t_ac_rd):
                    e.wait_ge(sAc.h, t)
                    e.tensor_tensor(out=recs[:, RRD:RRD + 512],
                                    in0=recs[:, RRD:RRD + 512],
                                    in1=recs[:, RDEN:RDEN + 512],
                                    op=ALU.mult).then_inc(sVe.h, 1)
                p.emit("vector", fn_rd2)

                t_mm_bd = sMM.take(1)

                def fn_bd(e, t=t_ve_rd):
                    e.wait_ge(sVe.h, t)
                    e.matmul(out=pb[6][:], lhsT=ones_row32[:],
                             rhs=recs[:, RRD:RRD + 512], start=True,
                             stop=True).then_inc(sMM.h, 1)
                p.emit("tensor", fn_bd)

                t_ve_bd = sVe.take(1)

                def fn_bd_ev(e, t=t_mm_bd):
                    e.wait_ge(sMM.h, t)
                    e.tensor_copy(out=rden_bc[:], in_=pb[6][:]).then_inc(sVe.h, 1)
                p.emit("vector", fn_bd_ev)

                t_cmb_prev_ec = 0
                for ec in range(2):
                    t_mm_amec = sMM.take(1)

                    def fn_am(e, ec=ec, t_sq=t_ve_sq1, t_mz=t_mz,
                              t_prev=t_cmb_prev_ec):
                        e.wait_ge(sVe.h, t_sq)
                        e.wait_ge(sMz.h, t_mz)
                        if t_prev:
                            e.wait_ge(sVe.h, t_prev)
                        pairs = [(memz_sb[d][:, ec * 128:(ec + 1) * 128],
                                  sq_sb[d][:]) for d in range(2)]
                        mm_chain(e, pb[5][:], pairs, sMM, 1)
                    p.emit("tensor", fn_am)

                    t_ve_cmb = sVe.take(2)
                    t_cmb_prev_ec = t_ve_cmb

                    def fn_cmb(e, ec=ec, h=h, t_am=t_mm_amec,
                               t_war=acst_store.get("last", 0)):
                        e.wait_ge(sMM.h, t_am)
                        if t_war:
                            e.wait_ge(sSt.h, t_war)
                        e.scalar_tensor_tensor(out=am_sb[:], in0=pb[5][:],
                                               scalar=gcols[:, h:h + 1],
                                               in1=rden_bc[:],
                                               op0=ALU.mult, op1=ALU.mult)
                        e.scalar_tensor_tensor(out=smask[:], in0=pb[2 + ec][:],
                                               scalar=gcols[:, H + h:H + h + 1],
                                               in1=rsum_bc[:],
                                               op0=ALU.mult, op1=ALU.mult)
                        e.tensor_tensor(out=ac_st[ec][:], in0=am_sb[:],
                                        in1=smask[:], op=ALU.add).then_inc(sVe.h, 2)
                    p.emit("vector", fn_cmb)

                    t_st_ac = sSt.take(16)
                    acst_store[ec] = t_st_ac
                    acst_store["last"] = t_st_ac

                    def fn_st_ac(e, ec=ec, h=h, s=s, t=t_ve_cmb):
                        e.wait_ge(sVe.h, t)
                        e.dma_start(out=attcat_d.ap()[(h * 2 + ec) * 128:
                                                      (h * 2 + ec + 1) * 128,
                                                      s * 512:(s + 1) * 512],
                                    in_=ac_st[ec][:]).then_inc(sSt.h, 16)
                    p.emit("scalar", fn_st_ac)

                slot_cmb_guard[0] = t_cmb_prev_ec
            t_head_end = sMM.n

        p.close_scope(p3_es)
        p.close_scope(att_es)
        p.barrier(sB)

        if "attcat" in dbg_out:
            t = sSt.take(16)

            def fn_dbga(e, tw=sSt.n - 16):
                e.wait_ge(sSt.h, tw)
                e.dma_start(out=dbg_out["attcat"], in_=attcat_d.ap()).then_inc(sSt.h, 16)
            p.emit("sync", fn_dbga)
        if stop_after == "p3":
            return _finish(nc, p, es, sSt, att_es, sorted(dbg_out.keys()))

        p46_es = p.open_scope()
        xn_own = [p.sb(f"xno{i}", [128, TOK], BF16, scope=p46_es)
                  for i in range(FC)]
        bo_bc = p.sb("bo_bc", [128, D], F32, scope=p46_es)
        b1sb = [p.sb(f"b1sb{i}", [128, 32], F32, scope=p46_es) for i in range(2)]
        b2bc_sb = p.sb("b2bc_sb", [128, ME * D], BF16, scope=p46_es)
        p4_es = p.open_scope()
        woT_sb = [p.sb(f"woT{i}", [128, D], BF16, scope=p4_es)
                  for i in range(FC)]
        attc = [p.sb(f"attc{i}", [128, FC * 128], BF16, scope=p4_es)
                for i in range(2)]

        t_p4w = sW4.take(16 * (FC + 1))
        t_b2bc = sW4.take(16)

        def p4_w(e):
            for i in range(FC):
                e.dma_start(out=woT_sb[i][:],
                            in_=woT[i * 128:(i + 1) * 128, :]).then_inc(sW4.h, 16)
            e.dma_start(out=bo_bc[:], in_=bo128).then_inc(sW4.h, 16)
            e.dma_start(out=b2bc_sb[:], in_=b2bc_in).then_inc(sW4.h, 16)
        p.emit("gpsimd", p4_w)

        t_p4xn = sLd.take(16 * FC)

        def p4_xn(e):
            for i in range(FC):
                e.dma_start(out=xn_own[i][:],
                            in_=xn_fm.ap()[i * 128:(i + 1) * 128, 0:TOK]).then_inc(sLd.h, 16)
        p.emit("sync", p4_xn)

        init_t = {}
        for t in range(NT):
            sLd.take(16)

            def fn_x0(e, t=t):
                e.dma_start(out=y_acc[:, t * D:(t + 1) * D],
                            in_=x0b[t * 128:(t + 1) * 128, :]).then_inc(sLd.h, 16)
            p.emit("sync", fn_x0)
        t_p4_all = sLd.n
        for t in range(NT):
            t_ve = sVe.take(1)
            init_t[t] = t_ve

            def fn_bo(e, t=t, t_ld=t_p4_all, t_w=t_b2bc):
                e.wait_ge(sLd.h, t_ld)
                e.wait_ge(sW4.h, t_w)
                e.tensor_tensor(out=y_acc[:, t * D:(t + 1) * D],
                                in0=y_acc[:, t * D:(t + 1) * D],
                                in1=bo_bc[:], op=ALU.add).then_inc(sVe.h, 1)
            p.emit("vector", fn_bo)

        ld_attc_t = {}
        wo_mm_t = {}
        acc_t = {}
        for t in range(NT):
            t_lda = (sLdA if t % 2 == 0 else sLdB).take(16 * FC)
            ld_attc_t[t] = t_lda

            def fn_lda(e, t=t, t_guard=wo_mm_t.get((t - 2, 1), 0)):
                if t_guard:
                    e.wait_ge(sMM.h, t_guard)
                psem = sLdA if t % 2 == 0 else sLdB
                for i in range(FC):
                    e.dma_start(out=attc[t % 2][:, i * 128:(i + 1) * 128],
                                in_=attcat_d.ap()[i * 128:(i + 1) * 128,
                                                  t * 128:(t + 1) * 128]).then_inc(psem.h, 16)
            p.emit("sync", fn_lda)
            for oc in range(2):
                t_mm = sMM.take(1)
                wo_mm_t[(t, oc)] = t_mm

                def fn_mm(e, t=t, oc=oc, t_lda=t_lda, psem=None,
                          t_w=t_b2bc,
                          t_bfree=acc_t.get((t - 1, oc), 0),
                          first=(t == 0 and oc == 0)):
                    if first:
                        e.wait_ge(sW4.h, t_w)
                    e.wait_ge((sLdA if t % 2 == 0 else sLdB).h, t_lda)
                    if t_bfree:
                        e.wait_ge(sVe.h, t_bfree)
                    pairs = [(attc[t % 2][:, i * 128:(i + 1) * 128],
                              woT_sb[i][:, oc * 512:(oc + 1) * 512])
                             for i in range(FC)]
                    mm_chain(e, pb[4 + oc][:], pairs, sMM, 1)
                p.emit("tensor", fn_mm)

                t_ve = sVe.take(1)
                acc_t[(t, oc)] = t_ve

                def fn_acc(e, t=t, oc=oc, t_mm=t_mm, t_init=init_t[t]):
                    e.wait_ge(sMM.h, t_mm)
                    e.wait_ge(sVe.h, t_init)
                    sl = y_acc[:, t * D + oc * 512:t * D + (oc + 1) * 512]
                    e.scalar_tensor_tensor(out=sl, in0=pb[4 + oc][:],
                                           scalar=1.0, in1=sl, op0=ALU.mult,
                                           op1=ALU.add).then_inc(sVe.h, 1)
                p.emit("vector", fn_acc)

        p.close_scope(p4_es)

        if stop_after == "p4":
            if "yacc1" in dbg_out:
                for t in range(NT):
                    tt_ = sSt.take(16)

                    def fn_dy(e, t=t, tv=sVe.n, ta=sAc.n):
                        e.wait_ge(sVe.h, tv)
                        e.wait_ge(sAc.h, ta)
                        e.dma_start(out=dbg_out["yacc1"][t * 128:(t + 1) * 128, :],
                                    in_=y_acc[:, t * D:(t + 1) * D]).then_inc(sSt.h, 16)
                    p.emit("sync", fn_dy)
            return _finish(nc, p, es, sSt, att_es, sorted(dbg_out.keys()))

        p5_es = p.open_scope()
        gt = p.sb("gt", [128, NT * 8], F32, scope=p5_es)
        lgdbg = p.sb("lgdbg", [128, NT * 8], F32, scope=p5_es)
        gsc5 = p.sb("gsc5", [128, NT * 8], F32, scope=p5_es)
        gcol5 = p.sb("gcol5", [128, 4 * NT], F32, scope=p5_es)
        selb = p.sb("selb", [128, 8], BF16, scope=p5_es)

        def C5(t, k):
            return gcol5[:, t * 4 + k:t * 4 + k + 1]

        lg_mm_t = {}
        bgadd_t = {}
        for t in range(NT):
            t_mm = sMM.take(1)
            lg_mm_t[t] = t_mm

            def fn_lg(e, t=t, t_xn=t_p4_all, t_w=t_w0,
                      t_bfree=bgadd_t.get(t - 2, 0)):
                e.wait_ge(sLd.h, t_xn)
                e.wait_ge(sW0.h, t_w)
                if t_bfree:
                    e.wait_ge(sVe.h, t_bfree)
                pairs = [(xn_own[i][:, t * 128:(t + 1) * 128],
                          wg_sb[:, i * 8:(i + 1) * 8]) for i in range(FC)]
                mm_chain(e, pb[6][:, 0:8] if t % 2 == 0 else pb[3][:, 0:8],
                         pairs, sMM, 1)
            p.emit("tensor", fn_lg)

            t_bg = sVe.take(1)
            bgadd_t[t] = t_bg

            def fn_bgadd(e, t=t, t_mm=t_mm):
                e.wait_ge(sMM.h, t_mm)
                bank = pb[6] if t % 2 == 0 else pb[3]
                e.tensor_tensor(out=gt[:, t * 8:(t + 1) * 8],
                                in0=bank[:, 0:8], in1=bg_bc[:],
                                op=ALU.add).then_inc(sVe.h, 1)
            p.emit("vector", fn_bgadd)

        import os as _os
        _kp5 = int(_os.environ.get("KP5", "9"))
        sel_done_t = {}
        for t0 in range(0, NT, 2):
            ts_pair = [t0, t0 + 1]
            t_ve_a = sVe.take(4)

            def fn_v_a(e, ts_pair=tuple(ts_pair)):
                for t in ts_pair:
                    e.wait_ge(sVe.h, bgadd_t[t])
                for t in ts_pair:
                    e.tensor_copy(out=lgdbg[:, t * 8:(t + 1) * 8],
                                  in_=gt[:, t * 8:(t + 1) * 8])
                last = None
                for t in ts_pair:
                    last = e.tensor_reduce(out=C5(t, 0),
                                           in_=gt[:, t * 8:(t + 1) * 8],
                                           axis=AX.X, op=ALU.max, negate=True)
                last.then_inc(sVe.h, 4)
            p.emit("vector", fn_v_a)
            if _kp5 < 2:
                continue

            t_ac_a = sAc.take(2)

            def fn_s_a(e, ts_pair=tuple(ts_pair), t_v=t_ve_a):
                e.wait_ge(sVe.h, t_v)
                for t in ts_pair:
                    e.activation(out=gt[:, t * 8:(t + 1) * 8],
                                 in_=gt[:, t * 8:(t + 1) * 8], func=AF.Exp,
                                 bias=C5(t, 0), accum_out=C5(t, 1))
                for t in ts_pair:
                    e.activation(out=C5(t, 1), in_=C5(t, 1), func=AF.Ln)
                last = None
                for t in ts_pair:
                    last = e.activation(out=C5(t, 1), in_=C5(t, 1), func=AF.Exp,
                                        scale=-1.0)
                last.then_inc(sAc.h, 2)
            p.emit("scalar", fn_s_a)
            if _kp5 < 3:
                continue

            t_ve_b = sVe.take(2)

            def fn_v_b(e, ts_pair=tuple(ts_pair), t_a=t_ac_a):
                e.wait_ge(sAc.h, t_a)
                for t in ts_pair:
                    e.tensor_scalar(out=gt[:, t * 8:(t + 1) * 8],
                                    in0=gt[:, t * 8:(t + 1) * 8],
                                    scalar1=C5(t, 1), scalar2=None, op0=ALU.mult)
                for t in ts_pair:
                    e.tensor_reduce(out=C5(t, 0), in_=gt[:, t * 8:(t + 1) * 8],
                                    axis=AX.X, op=ALU.max, negate=True)
                for t in ts_pair:
                    e.tensor_scalar(out=gsc5[:, t * 8:(t + 1) * 8],
                                    in0=gt[:, t * 8:(t + 1) * 8],
                                    scalar1=C5(t, 0), scalar2=0.0,
                                    op0=ALU.add, op1=ALU.is_ge)
                for t in ts_pair:
                    e.scalar_tensor_tensor(out=gsc5[:, t * 8:(t + 1) * 8],
                                           in0=gsc5[:, t * 8:(t + 1) * 8],
                                           scalar=-1e9, in1=gt[:, t * 8:(t + 1) * 8],
                                           op0=ALU.mult, op1=ALU.add)
                for t in ts_pair:
                    e.tensor_reduce(out=C5(t, 2), in_=gsc5[:, t * 8:(t + 1) * 8],
                                    axis=AX.X, op=ALU.max, negate=True)
                for t in ts_pair:
                    e.tensor_scalar(out=gsc5[:, t * 8:(t + 1) * 8],
                                    in0=gt[:, t * 8:(t + 1) * 8],
                                    scalar1=C5(t, 2), scalar2=0.0,
                                    op0=ALU.add, op1=ALU.is_ge)
                for t in ts_pair:
                    e.tensor_tensor(out=gsc5[:, t * 8:(t + 1) * 8],
                                    in0=gsc5[:, t * 8:(t + 1) * 8],
                                    in1=gt[:, t * 8:(t + 1) * 8], op=ALU.mult)
                for t in ts_pair:
                    e.tensor_reduce(out=C5(t, 3),
                                    in_=gsc5[:, t * 8:(t + 1) * 8],
                                    axis=AX.X, op=ALU.add)
                last = None
                for t in ts_pair:
                    last = e.tensor_scalar(out=C5(t, 3), in0=C5(t, 3),
                                           scalar1=1e-8, scalar2=None,
                                           op0=ALU.add)
                last.then_inc(sVe.h, 2)
            p.emit("vector", fn_v_b)
            if _kp5 < 4:
                continue

            t_ac_b = sAc.take(2)

            def fn_s_b(e, ts_pair=tuple(ts_pair), t_v=t_ve_b):
                e.wait_ge(sVe.h, t_v)
                for t in ts_pair:
                    e.activation(out=C5(t, 3), in_=C5(t, 3), func=AF.Ln)
                last = None
                for t in ts_pair:
                    last = e.activation(out=C5(t, 3), in_=C5(t, 3), func=AF.Exp,
                                        scale=-1.0)
                last.then_inc(sAc.h, 2)
            p.emit("scalar", fn_s_b)
            if _kp5 < 5:
                continue

            for t in ts_pair:
                t_ve_c = sVe.take(1)
                sel_done_t[t] = t_ve_c

                def fn_v_c(e, t=t, t_a=t_ac_b):
                    e.wait_ge(sAc.h, t_a)
                    e.tensor_scalar(out=sel_all[:, t * 8:(t + 1) * 8],
                                    in0=gsc5[:, t * 8:(t + 1) * 8],
                                    scalar1=C5(t, 3), scalar2=None,
                                    op0=ALU.mult).then_inc(sVe.h, 1)
                p.emit("vector", fn_v_c)

                t_ve_e = sVe.take(1)

                def fn_b2acc(e, t=t, t_w=t_b2bc, t_init=init_t[t]):
                    e.wait_ge(sW4.h, t_w)
                    e.wait_ge(sVe.h, t_init)
                    last = None
                    for eix2 in range(ME):
                        for oc in range(2):
                            sl = y_acc[:, t * D + oc * 512:t * D + (oc + 1) * 512]
                            last = e.scalar_tensor_tensor(
                                out=sl,
                                in0=b2bc_sb[:, eix2 * D + oc * 512:
                                            eix2 * D + (oc + 1) * 512],
                                scalar=sel_all[:, t * 8 + eix2:t * 8 + eix2 + 1],
                                in1=sl, op0=ALU.mult, op1=ALU.add)
                    last.then_inc(sVe.h, 1)
                p.emit("vector", fn_b2acc)

        p.close_scope(p5_es)
        p.barrier(sB)

        if "yacc1" in dbg_out:
            for t in range(NT):
                tt_ = sSt.take(16)

                def fn_dy(e, t=t):
                    e.dma_start(out=dbg_out["yacc1"][t * 128:(t + 1) * 128, :],
                                in_=y_acc[:, t * D:(t + 1) * D]).then_inc(sSt.h, 16)
                p.emit("sync", fn_dy)
        if "sel" in dbg_out:
            tt_ = sSt.take(16)

            def fn_ds(e):
                e.dma_start(out=dbg_out["sel"], in_=sel_all[:]).then_inc(sSt.h, 16)
            p.emit("sync", fn_ds)
        if stop_after == "p5":
            return _finish(nc, p, es, sSt, att_es, sorted(dbg_out.keys()))

        p6_es = p.open_scope()
        w1h = [p.sb(f"w1h{i}", [128, TOK], BF16, scope=p6_es) for i in range(FC)]
        w2h = [p.sb(f"w2h{i}", [128, D], BF16, scope=p6_es) for i in range(16)]
        hfm = [p.sb(f"hfm{i}", [128, 512], BF16, scope=p6_es) for i in range(16)]

        w1bank_gelu = {0: 0, 1: 0}
        w2bank_ve = {0: 0, 1: 0}
        w2_end_prev_tqc = 0
        moe_mm_end = 0
        gelu_last_of_expert = {}
        for eix in range(ME):
            for half in range(2):
                t_guard_mm = moe_mm_end
                t_w1 = sWmoe.take(16 * FC)
                t_w2 = sWmoe.take(16 * 16)

                def fn_w(e, eix=eix, half=half, t_g=t_guard_mm):
                    if t_g:
                        e.wait_ge(sMM.h, t_g)
                    for i in range(FC):
                        e.dma_start(out=w1h[i][:],
                                    in_=w1[eix, i * 128:(i + 1) * 128,
                                           half * TOK:(half + 1) * TOK]).then_inc(sWmoe.h, 16)
                    for hc in range(16):
                        e.dma_start(out=w2h[hc][:],
                                    in_=w2[eix, half * TOK + hc * 128:
                                           half * TOK + (hc + 1) * 128,
                                           :]).then_inc(sWmoe.h, 16)
                p.emit("gpsimd", fn_w)

                if half == 0:
                    t_b1g = gelu_last_of_expert.get(eix - 2, 0)
                    t_b1 = sB1.take(16 * 32)

                    def fn_b1(e, eix=eix, t_g=t_b1g):
                        if t_g:
                            e.wait_ge(sAc.h, t_g)
                        for hc in range(32):
                            e.dma_start(
                                out=b1sb[eix % 2][:, hc:hc + 1],
                                in_=b1[eix, hc * 128:(hc + 1) * 128].rearrange(
                                    "(a p) -> p a", p=128)).then_inc(sB1.h, 16)
                    p.emit("gpsimd", fn_b1)

                for tqc in range(4):
                    gelu_t = {}
                    for hc in range(16):
                        bank = hc % 2
                        t_mm = sMM.take(1)

                        def fn_h(e, eix=eix, half=half, tqc=tqc, hc=hc,
                                 bank=bank, t_w2_full=t_w2,
                                 t_bfree=w1bank_gelu[bank],
                                 first=(tqc == 0 and hc == 0)):
                            if first:
                                e.wait_ge(sWmoe.h, t_w2_full)
                            if t_bfree:
                                e.wait_ge(sAc.h, t_bfree)
                            pairs = [(w1h[i][:, hc * 128:(hc + 1) * 128],
                                      xn_own[i][:, tqc * 512:(tqc + 1) * 512])
                                     for i in range(FC)]
                            mm_chain(e, pb[bank][:], pairs, sMM, 1)
                        p.emit("tensor", fn_h)

                        t_ac = sAc.take(1)
                        gelu_t[hc] = t_ac
                        w1bank_gelu[bank] = t_ac

                        def fn_g(e, eix=eix, half=half, hc=hc, bank=bank,
                                 t_mm=t_mm, t_w2prev=w2_end_prev_tqc,
                                 t_b1=t_b1):
                            e.wait_ge(sMM.h, t_mm)
                            if t_w2prev:
                                e.wait_ge(sMM.h, t_w2prev)
                            if hc == 0:
                                e.wait_ge(sB1.h, t_b1)
                            e.activation(out=hfm[hc][:], in_=pb[bank][:],
                                         func=AF.Gelu_apprx_tanh,
                                         bias=b1sb[eix % 2][:, half * 16 + hc:
                                                            half * 16 + hc + 1]
                                         ).then_inc(sAc.h, 1)
                        p.emit("scalar", fn_g)
                    gelu_end_tqc = gelu_t[15]
                    gelu_last_of_expert[eix] = gelu_end_tqc

                    gidx = 0
                    for tqs in range(4):
                        for fc2 in range(2):
                            bank = 2 + gidx % 2
                            t_mm = sMM.take(1)

                            def fn_y(e, tqs=tqs, fc2=fc2, tqc=tqc, bank=bank,
                                     t_gelu=gelu_end_tqc, t_w2=t_w2,
                                     t_bfree=w2bank_ve[bank - 2],
                                     first=(tqc == 0 and tqs == 0 and fc2 == 0)):
                                e.wait_ge(sAc.h, t_gelu)
                                if first:
                                    e.wait_ge(sWmoe.h, t_w2)
                                if t_bfree:
                                    e.wait_ge(sVe.h, t_bfree)
                                pairs = [(hfm[hc][:, tqs * 128:(tqs + 1) * 128],
                                          w2h[hc][:, fc2 * 512:(fc2 + 1) * 512])
                                         for hc in range(16)]
                                mm_chain(e, pb[bank][:], pairs, sMM, 1)
                            p.emit("tensor", fn_y)

                            t_ve = sVe.take(1)
                            w2bank_ve[bank - 2] = t_ve

                            def fn_a(e, eix=eix, tqc=tqc, tqs=tqs, fc2=fc2,
                                     bank=bank, t_mm=t_mm):
                                e.wait_ge(sMM.h, t_mm)
                                t = tqc * 4 + tqs
                                sl = y_acc[:, t * D + fc2 * 512:
                                           t * D + (fc2 + 1) * 512]
                                e.scalar_tensor_tensor(
                                    out=sl, in0=pb[bank][:],
                                    scalar=sel_all[:, t * 8 + eix:t * 8 + eix + 1],
                                    in1=sl, op0=ALU.mult,
                                    op1=ALU.add).then_inc(sVe.h, 1)
                            p.emit("vector", fn_a)
                            gidx += 1
                    w2_end_prev_tqc = sMM.n
                moe_mm_end = sMM.n

        p.close_scope(p6_es)
        p.barrier(sB)

        if "yacc2" in dbg_out:
            for t in range(NT):
                tt_ = sSt.take(16)

                def fn_dy2(e, t=t):
                    e.dma_start(out=dbg_out["yacc2"][t * 128:(t + 1) * 128, :],
                                in_=y_acc[:, t * D:(t + 1) * D]).then_inc(sSt.h, 16)
                p.emit("sync", fn_dy2)
        if stop_after == "p6":
            return _finish(nc, p, es, sSt, att_es, sorted(dbg_out.keys()))

        p789_es = p.open_scope()
        xn1_fm_sb = [p.sb(f"x1f{i}", [128, TOK], BF16, scope=p789_es)
                     for i in range(FC)]
        l1s_sb = [p.sb(f"l1s{i}", [128, D], BF16, scope=p789_es)
                  for i in range(FC)]
        p7_es = p.open_scope()
        ln1_o = p.sb("ln1_o", [128, 8 * D], BF16, scope=p7_es)
        ln1_c = p.sb("ln1_c", [128, 64], F32, scope=p7_es)
        l1w32 = p.sb("l1w32", [128, D], F32, scope=p7_es)

        sign_t = 0
        for i in range(FC):
            t_ld = sLd.take(16)

            def fn_lw(e, i=i, t_g=sign_t):
                if t_g:
                    e.wait_ge(sAc.h, t_g)
                e.dma_start(out=l1w32[:],
                            in_=l1wT[i * 128:(i + 1) * 128, :]).then_inc(sLd.h, 16)
            p.emit("sync", fn_lw)
            sign_t = sAc.take(1)

            def fn_sg(e, i=i, t_ld=t_ld):
                e.wait_ge(sLd.h, t_ld)
                e.activation(out=l1s_sb[i][:], in_=l1w32[:],
                             func=AF.Sign).then_inc(sAc.h, 1)
            p.emit("scalar", fn_sg)

        ln1_st = {}
        t_p6_ve = sVe.n
        for grp in range(2):
            t_ac = sAc.take(1)

            def fn_ln1(e, grp=grp, t_ve=t_p6_ve):
                e.wait_ge(sVe.h, t_ve)
                SUM, SSQ, M, M2, V, RS, BC, SC = range(8)

                def cc(ti, kk):
                    return ln1_c[:, ti * 8 + kk:ti * 8 + kk + 1]
                for ti in range(8):
                    t = grp * 8 + ti
                    xt = y_acc[:, t * D:(t + 1) * D]
                    e.activation(out=ln1_o[:, ti * D:(ti + 1) * D], in_=xt,
                                 func=AF.Square, accum_out=cc(ti, SSQ))
                    e.activation(out=ln1_o[:, ti * D:(ti + 1) * D], in_=xt,
                                 func=AF.Identity, accum_out=cc(ti, SUM))
                for ti in range(8):
                    e.mul(out=cc(ti, M), in_=cc(ti, SUM), mul=1.0 / D)
                for ti in range(8):
                    e.activation(out=cc(ti, M2), in_=cc(ti, M), func=AF.Square)
                for ti in range(8):
                    e.activation(out=cc(ti, SC), in_=cc(ti, M2),
                                 func=AF.Identity, scale=-1.0, bias=eps_col[:])
                for ti in range(8):
                    e.activation(out=cc(ti, V), in_=cc(ti, SSQ),
                                 func=AF.Identity, scale=1.0 / D, bias=cc(ti, SC))
                for ti in range(8):
                    e.activation(out=cc(ti, V), in_=cc(ti, V), func=AF.Ln)
                for ti in range(8):
                    e.activation(out=cc(ti, RS), in_=cc(ti, V), func=AF.Exp,
                                 scale=-0.5)
                for ti in range(8):
                    e.activation(out=cc(ti, SC), in_=cc(ti, M),
                                 func=AF.Identity, scale=cc(ti, RS))
                for ti in range(8):
                    e.mul(out=cc(ti, BC), in_=cc(ti, SC), mul=-1.0)
                last = None
                for ti in range(8):
                    t = grp * 8 + ti
                    last = e.activation(out=ln1_o[:, ti * D:(ti + 1) * D],
                                        in_=y_acc[:, t * D:(t + 1) * D],
                                        func=AF.Identity, scale=cc(ti, RS),
                                        bias=cc(ti, BC))
                last.then_inc(sAc.h, 1)
            p.emit("scalar", fn_ln1)

            t_st = sSt.take(16 * 8)
            ln1_st[grp] = t_st

            def fn_st1(e, grp=grp, t_ac=t_ac):
                e.wait_ge(sAc.h, t_ac)
                for ti in range(8):
                    t = grp * 8 + ti
                    e.dma_start(out=xn1_tm.ap()[t * 128:(t + 1) * 128, :],
                                in_=ln1_o[:, ti * D:(ti + 1) * D]).then_inc(sSt.h, 16)
            p.emit("sync", fn_st1)
            if grp == 0:
                def fn_g1(e, t=t_st):
                    e.wait_ge(sSt.h, t)
                t_g1 = t_st

                def fn_w1(e, t=t_g1):
                    e.wait_ge(sSt.h, t)
                p.emit("scalar", fn_w1)

        t_tr1 = {}
        for i in range(FC):
            t_tr = sLd.take(16)
            t_tr1[i] = t_tr

            def fn_tr1(e, i=i, t_pv=ln1_st[1]):
                e.wait_ge(sSt.h, t_pv)
                e.dma_start(out=xn1_fm_sb[i][:],
                            in_=xn1_tm.ap()[:, i * 128:(i + 1) * 128],
                            transpose=True).then_inc(sLd.h, 16)
            p.emit("sync", fn_tr1)
        t_tr_all = t_tr1[FC - 1]
        p.close_scope(p7_es)

        if "xn1" in dbg_out:
            tt_ = sSt.take(16)

            def fn_dx1(e, tw=ln1_st[1]):
                e.wait_ge(sSt.h, tw)
                e.dma_start(out=dbg_out["xn1"], in_=xn1_tm.ap()).then_inc(sSt.h, 16)
            p.emit("sync", fn_dx1)
        if stop_after == "p7":
            return _finish(nc, p, es, sSt, att_es, sorted(dbg_out.keys()))

        p.barrier(sB)

        p8_es = p.open_scope()
        bd_sb = {g: [p.sb(f"bd{g}{i}", [128, 128], BF16, scope=p8_es)
                     for i in range(FC)] for g in "fih"}
        gs8 = [p.sb(f"gs8_{i}", [128, 512], F32, scope=p8_es) for i in range(7)]
        fv_st = [p.sb(f"fv_st{i}", [128, 512], F32, scope=p8_es)
                 for i in range(2)]

        t_bd = sBd.take(16 * 3 * FC)

        def fn_bd_ld(e):
            for g, srcw in (("f", wf_bd), ("i", wi_bd), ("h", wh_bd)):
                for i in range(FC):
                    e.dma_start(out=bd_sb[g][i][:], in_=srcw[i]).then_inc(sBd.h, 16)
        p.emit("gpsimd", fn_bd_ld)

        fv_store_t = {}
        p8_last_psum_read = [0]
        for i in range(FC):
            for tc in range(4):
                t_mm = sMM.take(1)

                def fn_gmm(e, i=i, tc=tc, t_w=t_bd, t_tr=t_tr_all,
                           t_bfree=p8_last_psum_read[0],
                           first=(i == 0 and tc == 0)):
                    if first:
                        e.wait_ge(sBd.h, t_w)
                        e.wait_ge(sLd.h, t_tr)
                    if t_bfree:
                        e.wait_ge(sVe.h, t_bfree)
                    last = None
                    for gi_, g in enumerate("fih"):
                        last = e.matmul(out=pb[gi_][:], lhsT=bd_sb[g][i][:],
                                        rhs=xn1_fm_sb[i][:, tc * 512:(tc + 1) * 512],
                                        start=True, stop=True)
                    last.then_inc(sMM.h, 1)
                p.emit("tensor", fn_gmm)

                t_ve1_ = sVe.take(1)

                def fn_e_v1(e, t_mm=t_mm, t_war=fv_store_t.get((i, tc, "war"), 0),
                            war=fv_store_t.get("last", 0)):
                    e.wait_ge(sMM.h, t_mm)
                    if war:
                        e.wait_ge(sSt.h, war)
                    e.tensor_scalar(out=gs8[1][:], in0=pb[0][:], scalar1=-1.0,
                                    scalar2=0.0, op0=ALU.mult, op1=ALU.max)
                    e.tensor_scalar(out=gs8[3][:], in0=pb[1][:], scalar1=-1.0,
                                    scalar2=0.0, op0=ALU.mult, op1=ALU.max)
                    e.tensor_scalar(out=gs8[6][:], in0=pb[2][:], scalar1=-1.0,
                                    scalar2=0.0, op0=ALU.mult,
                                    op1=ALU.max).then_inc(sVe.h, 1)
                p.emit("vector", fn_e_v1)

                t_ac1_ = sAc.take(1)

                def fn_e_s1(e, t_mm=t_mm, t_v=t_ve1_):
                    e.wait_ge(sMM.h, t_mm)
                    e.wait_ge(sVe.h, t_v)
                    e.activation(out=gs8[0][:], in_=pb[0][:], func=AF.Abs)
                    e.activation(out=gs8[2][:], in_=pb[1][:], func=AF.Abs)
                    e.activation(out=gs8[5][:], in_=pb[2][:], func=AF.Abs)
                    e.activation(out=gs8[0][:], in_=gs8[0][:], func=AF.Exp,
                                 scale=-1.0)
                    e.activation(out=gs8[2][:], in_=gs8[2][:], func=AF.Exp,
                                 scale=-1.0)
                    e.activation(out=gs8[5][:], in_=gs8[5][:], func=AF.Exp,
                                 scale=-1.0)
                    e.activation(out=gs8[0][:], in_=gs8[0][:], func=AF.Ln,
                                 bias=1.0)
                    e.activation(out=gs8[2][:], in_=gs8[2][:], func=AF.Ln,
                                 bias=1.0)
                    e.activation(out=gs8[5][:], in_=gs8[5][:], func=AF.Ln,
                                 bias=1.0).then_inc(sAc.h, 1)
                p.emit("scalar", fn_e_s1)

                t_ve2_ = sVe.take(1)

                def fn_e_v2(e, t_a=t_ac1_):
                    e.wait_ge(sAc.h, t_a)
                    e.tensor_tensor(out=gs8[1][:], in0=gs8[0][:], in1=gs8[1][:],
                                    op=ALU.add)
                    e.tensor_tensor(out=gs8[3][:], in0=gs8[2][:], in1=gs8[3][:],
                                    op=ALU.add)
                    e.tensor_tensor(out=gs8[6][:], in0=gs8[5][:], in1=gs8[6][:],
                                    op=ALU.add)
                    e.tensor_tensor(out=gs8[0][:], in0=gs8[1][:], in1=gs8[3][:],
                                    op=ALU.subtract)
                    e.tensor_scalar(out=gs8[1][:], in0=gs8[0][:], scalar1=0.0,
                                    scalar2=None, op0=ALU.max)
                    e.tensor_scalar(out=gs8[3][:], in0=gs8[0][:], scalar1=-1.0,
                                    scalar2=0.0, op0=ALU.mult,
                                    op1=ALU.max).then_inc(sVe.h, 1)
                p.emit("vector", fn_e_v2)

                t_ac2_ = sAc.take(1)

                def fn_e_s2(e, t_v=t_ve2_):
                    e.wait_ge(sVe.h, t_v)
                    e.activation(out=gs8[2][:], in_=gs8[0][:], func=AF.Abs)
                    e.activation(out=gs8[2][:], in_=gs8[2][:], func=AF.Exp,
                                 scale=-1.0)
                    e.activation(out=gs8[2][:], in_=gs8[2][:], func=AF.Ln,
                                 bias=1.0)
                    e.activation(out=gs8[6][:], in_=gs8[6][:], func=AF.Exp,
                                 scale=-1.0).then_inc(sAc.h, 1)
                p.emit("scalar", fn_e_s2)

                t_ve3_ = sVe.take(1)

                def fn_e_v3(e, t_a=t_ac2_):
                    e.wait_ge(sAc.h, t_a)
                    e.tensor_tensor(out=gs8[1][:], in0=gs8[2][:], in1=gs8[1][:],
                                    op=ALU.add)
                    e.tensor_tensor(out=gs8[3][:], in0=gs8[2][:], in1=gs8[3][:],
                                    op=ALU.add)
                    e.tensor_scalar(out=gs8[5][:], in0=pb[2][:], scalar1=0.5,
                                    scalar2=None, op0=ALU.add)
                    e.tensor_tensor(out=gs8[5][:], in0=gs8[5][:], in1=gs8[6][:],
                                    op=ALU.max).then_inc(sVe.h, 1)
                p.emit("vector", fn_e_v3)

                p8_last_psum_read[0] = t_ve3_
                t_ac3_ = sAc.take(1)

                def fn_e_s3(e, t_v=t_ve3_):
                    e.wait_ge(sVe.h, t_v)
                    e.activation(out=fv_st[0][:], in_=gs8[1][:], func=AF.Exp,
                                 scale=-1.0)
                    e.activation(out=gs8[3][:], in_=gs8[3][:], func=AF.Exp,
                                 scale=-1.0).then_inc(sAc.h, 1)
                p.emit("scalar", fn_e_s3)

                t_ve4_ = sVe.take(1)

                def fn_e_v4(e, t_a=t_ac3_):
                    e.wait_ge(sAc.h, t_a)
                    e.tensor_tensor(out=fv_st[1][:], in0=gs8[3][:], in1=gs8[5][:],
                                    op=ALU.mult).then_inc(sVe.h, 1)
                p.emit("vector", fn_e_v4)

                t_st_fv = sSt.take(32)
                fv_store_t["last"] = t_st_fv

                def fn_fv_st(e, i=i, tc=tc, t_v=t_ve4_, t_a=t_ac3_):
                    e.wait_ge(sVe.h, t_v)
                    e.dma_start(out=f_out[i * 128:(i + 1) * 128,
                                          tc * 512:(tc + 1) * 512],
                                in_=fv_st[0][:]).then_inc(sSt.h, 16)
                    e.dma_start(out=v_out[i * 128:(i + 1) * 128,
                                          tc * 512:(tc + 1) * 512],
                                in_=fv_st[1][:]).then_inc(sSt.h, 16)
                p.emit("scalar", fn_fv_st)

        p.close_scope(p8_es)

        f1_ve = {}
        for t in range(NT):
            for oc in range(2):
                gidx9 = t * 2 + oc
                bank = 4 + gidx9 % 2
                t_mm = sMM.take(1)

                def fn_f1(e, t=t, oc=oc, bank=bank, t_tr=t_tr_all,
                          t_sg=sign_t, t_bfree=f1_ve.get(gidx9 - 2, 0),
                          first=(t == 0 and oc == 0)):
                    if first:
                        e.wait_ge(sLd.h, t_tr)
                        e.wait_ge(sAc.h, t_sg)
                    if t_bfree:
                        e.wait_ge(sVe.h, t_bfree)
                    pairs = [(xn1_fm_sb[i][:, t * 128:(t + 1) * 128],
                              l1s_sb[i][:, oc * 512:(oc + 1) * 512])
                             for i in range(FC)]
                    mm_chain(e, pb[bank][:], pairs, sMM, 1)
                p.emit("tensor", fn_f1)

                t_ve = sVe.take(1)
                f1_ve[gidx9] = t_ve

                def fn_f1a(e, t=t, oc=oc, bank=bank, t_mm=t_mm):
                    e.wait_ge(sMM.h, t_mm)
                    sl = y_acc[:, t * D + oc * 512:t * D + (oc + 1) * 512]
                    e.tensor_tensor(out=sl, in0=pb[bank][:], in1=sl,
                                    op=ALU.add).then_inc(sVe.h, 1)
                p.emit("vector", fn_f1a)

                t_st = sSt.take(16)

                def fn_f1st(e, t=t, oc=oc, t_v=t_ve):
                    e.wait_ge(sVe.h, t_v)
                    e.dma_start(out=y_out[t * 128:(t + 1) * 128,
                                          oc * 512:(oc + 1) * 512],
                                in_=y_acc[:, t * D + oc * 512:
                                          t * D + (oc + 1) * 512]).then_inc(sSt.h, 16)
                p.emit("scalar", fn_f1st)

        p.close_scope(p789_es)
        p.close_scope(p46_es)
        return _finish(nc, p, es, sSt, att_es, sorted(dbg_out.keys()))


def _finish(nc, p, es, sSt, att_es, names):
    p.close_all_scopes()

    def fin(e, t=sSt.n):
        e.wait_ge(sSt.h, t)
    p.emit("sync", fin)
    block = es.enter_context(nc.Block())
    p.finalize(block)
    es.close()
    return nc, names


def build_launch_B():
    nc = bass.Bass("TRN2", target_bir_lowering=False, debug=False)
    fs = nc.dram_tensor("fs", [128, B * S], F32, kind="ExternalInput").ap()
    vs = nc.dram_tensor("vs", [128, B * S], F32, kind="ExternalInput").ap()
    ys = nc.dram_tensor("ys", [128, B * S], F32, kind="ExternalInput").ap()
    x2s = nc.dram_tensor("x2s", [128, B * S], F32, kind="ExternalOutput").ap()
    es = ExitStack()
    with es:
        fb = es.enter_context(nc.sbuf_tensor("fb", [128, S], F32))
        vb = es.enter_context(nc.sbuf_tensor("vb", [128, S], F32))
        yb = es.enter_context(nc.sbuf_tensor("yb", [128, S], F32))
        hb = es.enter_context(nc.sbuf_tensor("hb", [128, S], F32))
        sLd = es.enter_context(nc.semaphore("sLd"))
        sVe = es.enter_context(nc.semaphore("sVe"))
        sSt = es.enter_context(nc.semaphore("sSt"))
        block = es.enter_context(nc.Block())

        @block.sync
        def _(sync):
            for b in range(B):
                if b > 0:
                    sync.wait_ge(sVe, b)
                sync.dma_start(out=fb[:], in_=fs[:, b * S:(b + 1) * S]).then_inc(sLd, 16)
                sync.dma_start(out=vb[:], in_=vs[:, b * S:(b + 1) * S]).then_inc(sLd, 16)
                sync.dma_start(out=yb[:], in_=ys[:, b * S:(b + 1) * S]).then_inc(sLd, 16)
            sync.wait_ge(sSt, 16 * B)

        @block.vector
        def _(vector):
            for b in range(B):
                vector.wait_ge(sLd, 48 * (b + 1))
                if b > 0:
                    vector.wait_ge(sSt, 16 * b)
                vector.tensor_tensor_scan(out=hb[:], data0=fb[:], data1=vb[:],
                                          initial=0.0, op0=ALU.mult, op1=ALU.add)
                vector.tensor_tensor(out=hb[:], in0=hb[:], in1=yb[:],
                                     op=ALU.add).then_inc(sVe, 1)

        @block.scalar
        def _(scalar):
            for b in range(B):
                scalar.wait_ge(sVe, b + 1)
                scalar.dma_start(out=x2s[:, b * S:(b + 1) * S],
                                 in_=hb[:]).then_inc(sSt, 16)
    return nc


def run_launch_B(res_a):
    y_full = np.empty((B, S, D), np.float32)
    F_fm = np.empty((B, D, S), np.float32)
    V_fm = np.empty((B, D, S), np.float32)
    for c in range(NC):
        b = c // 4
        _, rows = _perm_for_core(c)
        own = rows[:TOK]
        y_full[b][own] = np.asarray(res_a[c]["y_out"])
        F_fm[b][:, own] = np.asarray(res_a[c]["f_out"])
        V_fm[b][:, own] = np.asarray(res_a[c]["v_out"])
    y_fm = np.ascontiguousarray(y_full.transpose(0, 2, 1))
    nc_b = build_launch_B()
    in_maps = []
    for c2 in range(NC):
        sl = slice(c2 * 128, (c2 + 1) * 128)
        in_maps.append({
            "fs": np.ascontiguousarray(
                F_fm[:, sl, :].transpose(1, 0, 2).reshape(128, B * S)),
            "vs": np.ascontiguousarray(
                V_fm[:, sl, :].transpose(1, 0, 2).reshape(128, B * S)),
            "ys": np.ascontiguousarray(
                y_fm[:, sl, :].transpose(1, 0, 2).reshape(128, B * S)),
        })
    res_b = run_bass_kernel_spmd(nc_b, in_maps, core_ids=list(range(NC)))
    x2 = np.empty((B, S, D), np.float32)
    for c2 in range(NC):
        sl = slice(c2 * 128, (c2 + 1) * 128)
        a = np.asarray(res_b.results[c2]["x2s"]).reshape(128, B, S)
        x2[:, :, sl] = a.transpose(1, 2, 0)
    return x2


def kernel(**inputs):
    per_core = prep_inputs_A(inputs)
    nc_a, _ = build_launch_A()
    res_a = run_bass_kernel_spmd(nc_a, per_core, core_ids=list(range(NC)))
    return run_launch_B(res_a.results)



def _perm_for_core(c):
    j = c % 4
    blocks = [j, 7 - j] + [b for b in range(8) if b not in (j, 7 - j)]
    rows = np.concatenate([np.arange(b * 1024, (b + 1) * 1024) for b in blocks])
    return blocks, rows


def _biasv_for_core(c):
    j = c % 4
    blocks, _ = _perm_for_core(c)
    true_chunk = np.concatenate([np.arange(b * 8, (b + 1) * 8) for b in blocks])
    out = np.full((4, 64), NEG, np.float32)
    for s in range(4):
        q0_true = true_chunk[4 * s] * 128
        for pos, vk in enumerate(SLOT_KEYPOS[s]):
            if pos < 4 or vk < 16:
                out[s, pos] = SHIFT
            else:
                tc = true_chunk[vk]
                out[s, pos] = SHIFT if (tc * 128) < q0_true else NEG
    return out.reshape(1, 256)


def prep_inputs_A(ins):
    x0 = np.ascontiguousarray(np.asarray(ins["L0_attn_hidden_states"],
                                         dtype=np.float32))
    com = {
        "wqT": np.ascontiguousarray(np.asarray(ins["L0_attn_Wq"]).T),
        "wkT": np.ascontiguousarray(np.asarray(ins["L0_attn_Wk"]).T),
        "wvT": np.ascontiguousarray(np.asarray(ins["L0_attn_Wv"]).T),
        "woT": np.ascontiguousarray(np.asarray(ins["L0_attn_Wo"]).T),
        "bo": np.ascontiguousarray(np.asarray(ins["L0_attn_bo"]).reshape(1, D)),
        "betas": np.ascontiguousarray(np.asarray(ins["L0_attn_betas"]).reshape(1, H)),
        "memz": np.ascontiguousarray(np.concatenate(
            [np.asarray(ins["L0_attn_mem"]), np.asarray(ins["L0_attn_z"])], axis=2)),
        "wg": np.ascontiguousarray(np.asarray(ins["L0_ffn_Wg"])),
        "bgv": np.ascontiguousarray(np.asarray(ins["L0_ffn_bg"]).reshape(1, 8)),
        "w1": np.ascontiguousarray(np.asarray(ins["L0_ffn_W1"])),
        "b1": np.ascontiguousarray(np.asarray(ins["L0_ffn_b1"])),
        "w2": np.ascontiguousarray(np.asarray(ins["L0_ffn_W2"])),
        "b2": np.ascontiguousarray(np.asarray(ins["L0_ffn_b2"])),
        "l1wT": np.ascontiguousarray(np.asarray(ins["L1_ffn_weight"]).T),
        "mask_in": np.ascontiguousarray(
            np.where(np.arange(896)[None, :] < np.arange(128)[:, None] + 384,
                     np.float32(NEG), np.float32(0.0)).astype(np.float32)),
        "eye128": np.eye(128, dtype=np.float32),
    }
    com["bg128"] = np.ascontiguousarray(
        np.broadcast_to(np.asarray(ins["L0_ffn_bg"], np.float32).reshape(1, 8),
                        (128, 8)).copy())
    com["bo128"] = np.ascontiguousarray(
        np.broadcast_to(np.asarray(ins["L0_attn_bo"], np.float32).reshape(1, D),
                        (128, D)).copy())
    com["b2bc_in"] = np.ascontiguousarray(
        np.broadcast_to(np.asarray(ins["L0_ffn_b2"], np.float32).reshape(1, ME * D),
                        (128, ME * D)).copy())
    for nm, key in (("wf_bd", "L1_attn_Wf"), ("wi_bd", "L1_attn_Wi"),
                    ("wh_bd", "L1_attn_Wh")):
        W = np.asarray(ins[key])
        bd = np.zeros((FC, 128, 128), np.float32)
        for g in range(128):
            fc, gl = divmod(g, 16)
            bd[fc, gl * 8:(gl + 1) * 8, gl * 8:(gl + 1) * 8] = W[g].T
        com[nm] = np.ascontiguousarray(bd)
    per_core = []
    for c in range(NC):
        b = c // 4
        _, rows = _perm_for_core(c)
        m = dict(com)
        m["x0b"] = np.ascontiguousarray(x0[b][rows])
        m["bias128"] = np.ascontiguousarray(
            np.broadcast_to(_biasv_for_core(c), (128, 256)).copy())
        per_core.append(m)
    return per_core



